# version 01
# baseline (speedup 1.0000x reference)
import numpy as np
import ml_dtypes
from contextlib import ExitStack

import concourse.bass as bass
import concourse.mybir as mybir
from concourse.bass_utils import run_bass_kernel_spmd

F32 = mybir.dt.float32
BF16 = mybir.dt.bfloat16
AF = mybir.ActivationFunctionType
ALU = mybir.AluOpType

D = 2048
LQ = 4096
T = 512
NT = LQ // T
DC = D // 128
NS = 16
NH = 16
EPS_LN = 1e-5
EPS_ATT = 1e-6

BF = ml_dtypes.bfloat16


_DTSZ = {F32: 4, BF16: 2, mybir.dt.float16: 2, mybir.dt.int32: 4, mybir.dt.int8: 1}


def _rng(ap):
    t = ap.tensor
    dims = list(ap.ap)
    is_dram = "DRam" in type(t).__name__
    use = dims if is_dram else dims[1:]
    ext = 1
    for s, c in use:
        ext += (c - 1) * abs(s)
    sz = _DTSZ.get(ap.dtype, 4)
    off = ap.offset
    return (t.name, off * sz, (off + ext) * sz)


class K:
    serial_dma = False
    NPOOL = 6

    def __init__(self, nc, stack):
        self.nc = nc
        self.e = {"pe": nc.tensor, "act": nc.scalar, "dve": nc.vector,
                  "gp": nc.gpsimd, "sync": nc.sync}
        self.sem = {}
        self.cnt = {}
        self.done = {eng: {} for eng in self.e}
        names = ["pe", "act", "dve", "gp"]
        names += [f"dma{i}" for i in range(self.NPOOL)]
        names += [f"gdma{i}" for i in range(self.NPOOL)]
        for name in names:
            self.sem[name] = stack.enter_context(nc.semaphore(name))
            self.cnt[name] = 0
        self.qidx = {"dma": 0, "gdma": 0}
        self.track = {}

    def inc(self, name, instr, n=1):
        instr.then_inc(self.sem[name], n)
        self.cnt[name] += n
        return (name, self.cnt[name])

    def wait(self, eng, tag):
        if tag is None:
            return
        name, val = tag
        if val <= 0:
            return
        if self.done[eng].get(name, 0) >= val:
            return
        self.e[eng].wait_ge(self.sem[name], val)
        self.done[eng][name] = val

    def waits(self, eng, tags):
        best = {}
        for t in tags:
            if t is None:
                continue
            n, v = t
            if v > best.get(n, -1):
                best[n] = v
        for n, v in best.items():
            self.wait(eng, (n, v))

    def _deps_read(self, r):
        name, lo, hi = r
        rec = self.track.get(name)
        if rec is None:
            return []
        return [t for (l, h, t) in rec["w"] if l < hi and lo < h]

    def _deps_write(self, r):
        name, lo, hi = r
        rec = self.track.get(name)
        if rec is None:
            return []
        out = [t for (l, h, t) in rec["w"] if l < hi and lo < h]
        out += [t for (l, h, t) in rec["r"] if l < hi and lo < h]
        return out

    def _note(self, kind, r, tag):
        name, lo, hi = r
        rec = self.track.setdefault(name, {"w": [], "r": []})
        if kind == "w":
            rec["w"] = [(l, h, t) for (l, h, t) in rec["w"] if not (lo <= l and h <= hi)]
            rec["r"] = [(l, h, t) for (l, h, t) in rec["r"] if not (lo <= l and h <= hi)]
            rec["w"].append((lo, hi, tag))
        else:
            rec["r"] = [(l, h, t) for (l, h, t) in rec["r"]
                        if not (l == lo and h == hi and t[0] == tag[0])]
            rec["r"].append((lo, hi, tag))

    def op(self, eng, fn, reads=(), writes=(), sem=None, skip_self=False):
        rr = [_rng(a) for a in reads if a is not None]
        ww = [_rng(a) for a in writes if a is not None]
        deps = []
        for r in rr:
            deps += self._deps_read(r)
        for r in ww:
            deps += self._deps_write(r)
        if skip_self:
            deps = [d for d in deps if d[0] != (sem or eng)]
        self.waits(eng, deps)
        instr = fn()
        tag = self.inc(sem or eng, instr, 16 if (sem and sem.startswith(("dma", "gdma"))) else 1)
        for r in rr:
            self._note("r", r, tag)
        for r in ww:
            self._note("w", r, tag)
        return tag

    def dma(self, qname, out, in_, **kw):
        eng = {"dma": "sync", "gdma": "gp"}[qname]
        i = self.qidx[qname]
        self.qidx[qname] = (i + 1) % (1 if self.serial_dma else self.NPOOL)
        sname = f"{qname}{i}"
        self.wait(eng, (sname, self.cnt[sname]))
        reads = [in_] + ([out] if kw.get("accum_op") not in (None, ALU.bypass) else [])
        return self.op(eng, lambda: self.e[eng].dma_start(out, in_, **kw),
                       reads=reads, writes=[out], sem=sname)

    def mm(self, out, lhsT, rhs, start, stop, bank=None):
        w = bank if (start and bank is not None) else out
        return self.op("pe", lambda: self.nc.tensor.matmul(out, lhsT, rhs,
                                                           start=start, stop=stop),
                       reads=[lhsT, rhs], writes=[w], skip_self=True)


def fdc(buf, dc, t0=0, t1=T):
    return buf[:, dc * T + t0: dc * T + t1]


def emit_ln(k, bufs, src, mode, out_f32=None, out_bf16=None, x_sb=None):
    nc = k.nc
    cast2 = bufs["cast2"]
    sq2 = bufs["sq2"]
    rowb = bufs["rowb"]
    psS = bufs["psS"]
    ones_bf = bufs["ones_bf"]
    ones_f1 = bufs["ones_f1"]

    for dc in range(DC):
        s = dc % 2
        cs = cast2[:, s * T:(s + 1) * T]
        qs = sq2[:, s * T:(s + 1) * T]
        k.op("dve", lambda cs=cs, dc=dc: nc.vector.tensor_copy(cs, fdc(src, dc)),
             reads=[fdc(src, dc)], writes=[cs])
        k.op("act", lambda qs=qs, dc=dc: nc.scalar.activation(qs, fdc(src, dc), AF.Square),
             reads=[fdc(src, dc)], writes=[qs])
        k.mm(psS[0:1, 0:T], ones_bf[:], cs, start=(dc == 0), stop=(dc == DC - 1),
             bank=psS[:, 0:512])
        k.mm(psS[0:1, 512:512 + T], ones_bf[:], qs, start=(dc == 0), stop=(dc == DC - 1),
             bank=psS[:, 512:1024])

    r_mean = rowb[0:1, 0:T]
    r_rstd = rowb[0:1, T:2 * T]
    r_mr = rowb[0:1, 2 * T:3 * T]
    k.op("act", lambda: nc.scalar.mul(r_mean, psS[0:1, 0:T], 1.0 / D),
         reads=[psS[0:1, 0:T]], writes=[r_mean])
    k.op("act", lambda: nc.scalar.mul(r_rstd, psS[0:1, 512:512 + T], 1.0 / D),
         reads=[psS[0:1, 512:512 + T]], writes=[r_rstd])
    k.op("dve", lambda: nc.vector.tensor_mul(r_mr, r_mean, r_mean),
         reads=[r_mean], writes=[r_mr])
    k.op("dve", lambda: nc.vector.tensor_sub(r_rstd, r_rstd, r_mr),
         reads=[r_rstd, r_mr], writes=[r_rstd])
    k.op("act", lambda: nc.scalar.activation(r_rstd, r_rstd, AF.Sqrt,
                                             bias=bufs["eps_sb"][0:1, 0:1]),
         reads=[r_rstd, bufs["eps_sb"][0:1, 0:1]], writes=[r_rstd])
    k.op("dve", lambda: nc.vector.reciprocal(r_rstd, r_rstd),
         reads=[r_rstd], writes=[r_rstd])
    k.op("dve", lambda: nc.vector.tensor_mul(r_mr, r_mean, r_rstd),
         reads=[r_mean, r_rstd], writes=[r_mr])

    k.mm(psS[:, 0:T], ones_f1[:], r_rstd, start=True, stop=True, bank=psS[:, 0:512])
    k.mm(psS[:, 512:512 + T], ones_f1[:], r_mr, start=True, stop=True,
         bank=psS[:, 512:1024])
    rb = bufs["rb_sb"]
    mrb = bufs["mrb_sb"]
    k.op("dve", lambda: nc.vector.tensor_copy(rb, psS[:, 0:T]),
         reads=[psS[:, 0:T]], writes=[rb])
    k.op("dve", lambda: nc.vector.tensor_copy(mrb, psS[:, 512:512 + T]),
         reads=[psS[:, 512:512 + T]], writes=[mrb])

    tmp = bufs["lnt"]
    last = None
    for dc in range(DC):
        s = dc % 2
        ts_ = tmp[:, s * T:(s + 1) * T]
        k.op("dve", lambda ts_=ts_, dc=dc: nc.vector.tensor_mul(ts_, fdc(src, dc), rb),
             reads=[fdc(src, dc), rb], writes=[ts_])
        if mode == "norm":
            if out_f32 is not None:
                last = k.op("dve", lambda ts_=ts_, dc=dc: nc.vector.tensor_sub(
                    fdc(out_f32, dc), ts_, mrb),
                    reads=[ts_, mrb], writes=[fdc(out_f32, dc)])
                if out_bf16 is not None:
                    last = k.op("dve", lambda dc=dc: nc.vector.tensor_copy(
                        fdc(out_bf16, dc), fdc(out_f32, dc)),
                        reads=[fdc(out_f32, dc)], writes=[fdc(out_bf16, dc)])
            else:
                last = k.op("dve", lambda ts_=ts_, dc=dc: nc.vector.tensor_sub(
                    fdc(out_bf16, dc), ts_, mrb),
                    reads=[ts_, mrb], writes=[fdc(out_bf16, dc)])
        else:
            k.op("dve", lambda ts_=ts_: nc.vector.tensor_sub(ts_, ts_, mrb),
                 reads=[ts_, mrb], writes=[ts_])
            last = k.op("gp", lambda ts_=ts_, dc=dc: nc.gpsimd.tensor_add(
                fdc(x_sb, dc), fdc(x_sb, dc), ts_),
                reads=[fdc(x_sb, dc), ts_], writes=[fdc(x_sb, dc)])
    return last


def emit_block_a(k, bufs, c):
    nc = k.nc
    x_sb = bufs["x_sb"]
    fU = bufs["fU"]; fDT = bufs["fDT"]; fY = bufs["fY"]
    psA, psB = bufs["psA"], bufs["psB"]
    psC = [bufs["psC0"], bufs["psC1"]]
    par = c % 2

    emit_ln(k, bufs, x_sb, "norm", out_f32=fU)

    ub = bufs["ub2"]
    dpar = bufs["dpar_sb"]
    xpw = bufs["xpw_sb"]
    for dc in range(DC):
        s = dc % 2
        ubs = ub[:, s * T:(s + 1) * T]
        k.op("dve", lambda dc=dc: nc.vector.tensor_scalar_mul(
            fdc(fY, dc), fdc(fU, dc), dpar[:, dc:dc + 1]),
            reads=[fdc(fU, dc), dpar[:, dc:dc + 1]], writes=[fdc(fY, dc)])
        k.op("dve", lambda ubs=ubs, dc=dc: nc.vector.tensor_copy(ubs, fdc(fU, dc)),
             reads=[fdc(fU, dc)], writes=[ubs])
        k.mm(psA[0:128, 0:T], xpw[:, dc * 160:dc * 160 + 128], ubs,
             start=(dc == 0), stop=(dc == DC - 1), bank=psA[:])
        k.mm(psB[0:32, 0:T], xpw[:, dc * 160 + 128:dc * 160 + 160], ubs,
             start=(dc == 0), stop=(dc == DC - 1), bank=psB[:])

    dtlow = bufs["dtlow"]
    bc_sb = bufs["bc_sb"]
    k.op("act", lambda: nc.scalar.copy(dtlow[:], psA[0:128, 0:T]),
         reads=[psA[0:128, 0:T]], writes=[dtlow[:]])
    k.op("dve", lambda: nc.vector.tensor_copy(bc_sb[:], psB[0:32, 0:T]),
         reads=[psB[0:32, 0:T]], writes=[bc_sb[:]])
    k.dma("dma", bufs["bc_dram"][par], bc_sb[:])

    dtw = bufs["dtw_sb"]
    dtb = bufs["dtb_sb"]
    tmpA = bufs["tmpA"]; tmpB = bufs["tmpB"]
    for dc in range(DC):
        s = dc % 2
        ps = psC[s]
        ta = tmpA[:, s * T:(s + 1) * T]
        tb = tmpB[:, s * T:(s + 1) * T]
        k.mm(ps[:, 0:T], dtw[:, dc * 128:(dc + 1) * 128], dtlow[:], start=True, stop=True)
        k.op("dve", lambda ta=ta, ps=ps, dc=dc: nc.vector.tensor_scalar_add(
            ta, ps[:, 0:T], dtb[:, dc:dc + 1]),
            reads=[ps[:, 0:T], dtb[:, dc:dc + 1]], writes=[ta])
        k.op("dve", lambda ta=ta, tb=tb: nc.vector.tensor_scalar(
            tb, ta, 0.0, None, ALU.max), reads=[ta], writes=[tb])
        k.op("act", lambda ta=ta: nc.scalar.activation(ta, ta, AF.Abs),
             reads=[ta], writes=[ta])
        k.op("act", lambda ta=ta: nc.scalar.activation(ta, ta, AF.Exp, scale=-1.0),
             reads=[ta], writes=[ta])
        k.op("act", lambda ta=ta: nc.scalar.activation(ta, ta, AF.Ln, bias=1.0),
             reads=[ta], writes=[ta])
        k.op("dve", lambda ta=ta, tb=tb, dc=dc: nc.vector.tensor_add(fdc(fDT, dc), ta, tb),
             reads=[ta, tb], writes=[fdc(fDT, dc)])

    for h in range(2):
        sl = slice(h * 8 * T, (h + 1) * 8 * T)
        k.op("gp", lambda sl=sl: nc.gpsimd.tensor_mul(fU[:, sl], fDT[:, sl], fU[:, sl]),
             reads=[fDT[:, sl], fU[:, sl]], writes=[fU[:, sl]])

    a_sb = bufs["a_sb"]
    hstate = bufs["hstate"]
    abuf = bufs["abuf"]
    bxb = bufs["bxb"]
    hbuf = bufs["hbuf"]
    bcb = bufs["bcb"]
    ccb = bufs["ccb"]
    for n in range(NS):
        sg = n % 2
        bslice = bcb[:, sg * T:(sg + 1) * T]
        cslice = ccb[:, sg * T:(sg + 1) * T]
        src_b = bufs["bc_dram"][par][n:n + 1, :].broadcast_to([128, T])
        src_c = bufs["bc_dram"][par][16 + n:17 + n, :].broadcast_to([128, T])
        k.dma("dma", bslice, src_b)
        k.dma("dma", cslice, src_c)
        for q in range(4):
            slot = q % 2
            bxs = bxb[:, slot * 4 * T:(slot + 1) * 4 * T]
            hbs = hbuf[:, slot * 4 * T:(slot + 1) * 4 * T]
            fus = fU[:, q * 4 * T:(q + 1) * 4 * T]
            k.op("gp", lambda bxs=bxs, fus=fus, bslice=bslice: nc.gpsimd.tensor_tensor(
                bxs.rearrange("p (dc t) -> p dc t", dc=4),
                fus.rearrange("p (dc t) -> p dc t", dc=4),
                bslice.unsqueeze(1).broadcast_to([128, 4, T]), ALU.mult),
                reads=[fus, bslice], writes=[bxs])
            for j in range(4):
                dc = q * 4 + j
                asl = dc % 2
                ab = abuf[:, asl * T:(asl + 1) * T]
                hcol = hstate[:, dc * NS + n: dc * NS + n + 1]
                hj = hbuf[:, (slot * 4 + j) * T:(slot * 4 + j + 1) * T]
                bxj = bxb[:, (slot * 4 + j) * T:(slot * 4 + j + 1) * T]
                k.op("act", lambda ab=ab, dc=dc, n=n: nc.scalar.activation(
                    ab, fdc(fDT, dc), AF.Exp,
                    scale=a_sb[:, dc * NS + n: dc * NS + n + 1]),
                    reads=[fdc(fDT, dc), a_sb[:, dc * NS + n: dc * NS + n + 1]],
                    writes=[ab])
                k.op("dve", lambda hj=hj, ab=ab, bxj=bxj, hcol=hcol: nc.vector.tensor_tensor_scan(
                    hj, ab, bxj, hcol, ALU.mult, ALU.add),
                    reads=[ab, bxj, hcol], writes=[hj])
                k.op("dve", lambda hcol=hcol, hj=hj: nc.vector.tensor_copy(
                    hcol, hj[:, T - 1:T]), reads=[hj[:, T - 1:T]], writes=[hcol])
            k.op("gp", lambda bxs=bxs, hbs=hbs, cslice=cslice: nc.gpsimd.tensor_tensor(
                bxs.rearrange("p (dc t) -> p dc t", dc=4),
                hbs.rearrange("p (dc t) -> p dc t", dc=4),
                cslice.unsqueeze(1).broadcast_to([128, 4, T]), ALU.mult),
                reads=[hbs, cslice], writes=[bxs])
            k.dma("gdma", fY[:, q * 4 * T:(q + 1) * 4 * T], bxs, accum_op=ALU.add)

    emit_ln(k, bufs, fY, "res", x_sb=x_sb)


def emit_block_b(k, bufs, io, c):
    nc = k.nc
    x_sb = bufs["x_sb"]; fY = bufs["fY"]
    u_bf = bufs["u_bf"]
    hf = bufs["hf"]
    wub = bufs["wub"]; wdb = bufs["wdb"]
    ups = bufs["ups_sb"]; dns = bufs["dns_sb"]; osc = bufs["os_sb"]
    psH = [bufs["psC0"], bufs["psC1"]]
    psO = [bufs["psA"], bufs["psB"]]

    emit_ln(k, bufs, x_sb, "norm", out_bf16=u_bf)

    for hc in range(64):
        s = hc % 2
        wt = wub[:, s * 2048:(s + 1) * 2048]
        src = io["wup_bf"][:, hc * 128:(hc + 1) * 128].rearrange("(dc p) f -> p dc f", p=128)
        k.dma("dma", wt.rearrange("p (dc f) -> p dc f", dc=DC), src)
        for dc in range(DC):
            k.mm(psH[s][:, 0:T], wt[:, dc * 128:(dc + 1) * 128], fdc(u_bf, dc),
                 start=(dc == 0), stop=(dc == DC - 1))
        hs = hf[:, hc * T:(hc + 1) * T]
        k.op("act", lambda hs=hs, s=s, hc=hc: nc.scalar.activation(
            hs, psH[s][:, 0:T], AF.Relu, scale=ups[:, hc:hc + 1]),
            reads=[psH[s][:, 0:T], ups[:, hc:hc + 1]], writes=[hs])

    for dco in range(DC):
        s = dco % 2
        wt = wdb[:, s * 8192:(s + 1) * 8192]
        src = io["wdn_bf"][:, dco * 128:(dco + 1) * 128].rearrange("(hc p) f -> p hc f", p=128)
        k.dma("dma", wt.rearrange("p (hc f) -> p hc f", hc=64), src)
        for hc in range(64):
            k.mm(psO[s][:, 0:T], wt[:, hc * 128:(hc + 1) * 128], hf[:, hc * T:(hc + 1) * T],
                 start=(hc == 0), stop=(hc == 63))
        k.op("dve", lambda s=s, dco=dco: nc.vector.tensor_scalar(
            fdc(fY, dco), psO[s][:, 0:T], dns[:, dco:dco + 1], osc[:, 0:1],
            ALU.mult, ALU.mult),
            reads=[psO[s][:, 0:T], dns[:, dco:dco + 1], osc[:, 0:1]],
            writes=[fdc(fY, dco)])

    emit_ln(k, bufs, fY, "res", x_sb=x_sb)


def emit_block_c(k, bufs, io, c):
    nc = k.nc
    x_sb = bufs["x_sb"]; fY = bufs["fY"]
    u_bf = bufs["u_bf"]
    qfb = bufs["qfb"]; kfb = bufs["kfb"]; att = bufs["att"]
    vtb = bufs["vtb"]
    kvst = bufs["kvst"]; kvst_bf = bufs["kvst_bf"]
    kt_sb = bufs["kt_sb"]
    wqb = bufs["wqb"]; wvb = bufs["wvb"]
    tR = bufs["tmpR"]; tE = bufs["tmpE"]
    zb = bufs["zb"]
    psA, psB, psS = bufs["psA"], bufs["psB"], bufs["psS"]
    psT = bufs["psT"]
    mask = bufs["mask_sb"]
    identb = bufs["ident_bf"]
    ones_bfr = bufs["ones_bf_row"]
    bq, bk, bo = bufs["bq_sb"], bufs["bk_sb"], bufs["bo_sb"]
    bv_bf = bufs["bv_bf"]

    emit_ln(k, bufs, x_sb, "norm", out_bf16=u_bf)

    for which, wsrc, bias, dstb in (("q", io["wq_bf"], bq, qfb), ("k", io["wk_bf"], bk, kfb)):
        for fc in range(NH):
            s = fc % 2
            wt = wqb[:, s * 2048:(s + 1) * 2048]
            src = wsrc[:, fc * 128:(fc + 1) * 128].rearrange("(dc p) f -> p dc f", p=128)
            k.dma("dma", wt.rearrange("p (dc f) -> p dc f", dc=DC), src)
            ps = psB[:, 0:512] if s == 0 else psA[:, 0:512]
            for dc in range(DC):
                k.mm(ps[:, 0:T], wt[:, dc * 128:(dc + 1) * 128], fdc(u_bf, dc),
                     start=(dc == 0), stop=(dc == DC - 1), bank=ps)
            tr = tR[:, 0:T]
            te = tE[:, 0:T]
            k.op("act", lambda tr=tr, ps=ps, fc=fc, bias=bias: nc.scalar.activation(
                tr, ps[:, 0:T], AF.Relu, bias=bias[:, fc:fc + 1]),
                reads=[ps[:, 0:T], bias[:, fc:fc + 1]], writes=[tr])
            k.op("act", lambda te=te, ps=ps, fc=fc, bias=bias: nc.scalar.activation(
                te, ps[:, 0:T], AF.Exp, bias=bias[:, fc:fc + 1]),
                reads=[ps[:, 0:T], bias[:, fc:fc + 1]], writes=[te])
            k.op("dve", lambda te=te, tr=tr, fc=fc, dstb=dstb: nc.vector.scalar_tensor_tensor(
                fdc(dstb, fc), te, 1.0, tr, ALU.min, ALU.add),
                reads=[te, tr], writes=[fdc(dstb, fc)])

    vps = [psA[:, 0:512], psA[:, 512:1024], psB[:, 0:512], psB[:, 512:1024]]
    for fg in range(4):
        for dh in range(2):
            src = io["wv_bf"][dh * 1024:(dh + 1) * 1024,
                              fg * 512:(fg + 1) * 512].rearrange("(dc p) f -> p dc f", p=128)
            k.dma("dma", wvb.rearrange("p (dc f) -> p dc f", dc=8, f=512), src)
            for cc in range(4):
                for dj in range(8):
                    dc = dh * 8 + dj
                    k.mm(vps[cc], u_bf[:, dc * T + cc * 128: dc * T + cc * 128 + 128],
                         wvb[:, dj * 512:(dj + 1) * 512],
                         start=(dc == 0), stop=False, bank=vps[cc])
        for cc in range(4):
            k.mm(vps[cc], ones_bfr[:], bv_bf[0:1, fg * 512:(fg + 1) * 512],
                 start=False, stop=True)
            dst = vtb[:, cc * 2064 + fg * 4 * 129: cc * 2064 + (fg * 4 + 4) * 129]
            k.op("dve", lambda dst=dst, cc=cc: nc.vector.tensor_copy(
                dst.rearrange("p (h e) -> p h e", h=4)[:, :, 0:128],
                vps[cc].rearrange("p (h e) -> p h e", h=4)),
                reads=[vps[cc]], writes=[dst])

    for cc in range(4):
        k.op("dve", lambda: nc.vector.tensor_copy(kvst_bf[:], kvst[:]),
             reads=[kvst[:]], writes=[kvst_bf[:]])
        vt_cc = vtb[:, cc * 2064:(cc + 1) * 2064]
        for h in range(NH):
            s = h % 2
            kslice = kfb[:, h * T + cc * 128: h * T + cc * 128 + 128]
            qslice = qfb[:, h * T + cc * 128: h * T + cc * 128 + 128]
            vt_h = vt_cc[:, h * 129:(h + 1) * 129]
            ktp = psT[:, s * 1024:s * 1024 + 128]
            kts = kt_sb[:, s * 128:(s + 1) * 128]
            k.op("pe", lambda ktp=ktp, kslice=kslice: nc.tensor.transpose(
                ktp, kslice, identb[:]), reads=[kslice, identb[:]], writes=[ktp])
            k.op("act", lambda kts=kts, ktp=ktp: nc.scalar.copy(kts, ktp),
                 reads=[ktp], writes=[kts])
            psk = psA[:, s * 512:s * 512 + 129]
            k.mm(psk, kts, vt_h, start=True, stop=True, bank=psA[:, s * 512:(s + 1) * 512])
            pss = psB[:, s * 512:s * 512 + 129]
            k.mm(pss, qslice, kvst_bf[:, h * 129:(h + 1) * 129], start=True, stop=False,
                 bank=psB[:, s * 512:(s + 1) * 512])
            psat = psS[:, 0:128]
            k.mm(psat, kslice, qslice, start=True, stop=True, bank=psS[:, 0:512])
            atb = bufs["atb"]
            k.op("dve", lambda psat=psat: nc.vector.tensor_mul(atb[:], psat, mask[:]),
                 reads=[psat, mask[:]], writes=[atb[:]])
            k.mm(pss, atb[:], vt_h, start=False, stop=True)
            k.op("dve", lambda psk=psk, h=h: nc.vector.tensor_add(
                kvst[:, h * 129:(h + 1) * 129], kvst[:, h * 129:(h + 1) * 129], psk),
                reads=[kvst[:, h * 129:(h + 1) * 129], psk],
                writes=[kvst[:, h * 129:(h + 1) * 129]])
            zs = zb[:, s:s + 1]
            k.op("dve", lambda zs=zs, pss=pss: nc.vector.tensor_scalar_add(
                zs, pss[:, 128:129], bufs["eps_sb"][:, 1:2]),
                reads=[pss[:, 128:129], bufs["eps_sb"][:, 1:2]], writes=[zs])
            k.op("dve", lambda zs=zs: nc.vector.reciprocal(zs, zs),
                 reads=[zs], writes=[zs])
            otb = bufs["otb"][:, s * 128:(s + 1) * 128]
            k.op("dve", lambda otb=otb, pss=pss, zs=zs: nc.vector.tensor_scalar_mul(
                otb, pss[:, 0:128], zs), reads=[pss[:, 0:128], zs], writes=[otb])
            otp = psT[:, s * 1024 + 128: s * 1024 + 256]
            k.op("pe", lambda otp=otp, otb=otb: nc.tensor.transpose(
                otp, otb, identb[:]), reads=[otb, identb[:]], writes=[otp])
            adst = att[:, h * T + cc * 128: h * T + cc * 128 + 128]
            k.op("act", lambda adst=adst, otp=otp: nc.scalar.copy(adst, otp),
                 reads=[otp], writes=[adst])

    for fc in range(NH):
        s = fc % 2
        wt = wqb[:, s * 2048:(s + 1) * 2048]
        src = io["wo_bf"][:, fc * 128:(fc + 1) * 128].rearrange("(ec p) f -> p ec f", p=128)
        k.dma("dma", wt.rearrange("p (ec f) -> p ec f", ec=DC), src)
        ps = psB[:, 0:512] if s == 0 else psA[:, 0:512]
        for ec in range(DC):
            k.mm(ps[:, 0:T], wt[:, ec * 128:(ec + 1) * 128], fdc(att, ec),
                 start=(ec == 0), stop=(ec == DC - 1), bank=ps)
        k.op("dve", lambda fc=fc, ps=ps: nc.vector.tensor_scalar_add(
            fdc(fY, fc), ps[:, 0:T], bo[:, fc:fc + 1]),
            reads=[ps[:, 0:T], bo[:, fc:fc + 1]], writes=[fdc(fY, fc)])

    emit_ln(k, bufs, fY, "res", x_sb=x_sb)


def emit_block_d(k, bufs, io, c):
    nc = k.nc
    x_sb = bufs["x_sb"]; fY = bufs["fY"]
    u_bf = bufs["u_bf"]
    hdn = bufs["hdn"]
    wgb = bufs["wgb"]; wub2 = bufs["wub2"]; wdb2 = bufs["wdb2"]
    gs = bufs["gs_sb"]; us = bufs["us_sb"]; ds = bufs["ds_sb"]
    psG = bufs["psC0"]; psU = bufs["psC1"]
    psO = [bufs["psA"], bufs["psB"]]
    sgt = bufs["sgt"]; upt = bufs["upt"]

    emit_ln(k, bufs, x_sb, "norm", out_bf16=u_bf)

    NI = 44
    for ic in range(NI):
        s = ic % 2
        wg = wgb[:, s * 2048:(s + 1) * 2048]
        wu = wub2[:, s * 2048:(s + 1) * 2048]
        srcg = io["wg_bf"][:, ic * 128:(ic + 1) * 128].rearrange("(dc p) f -> p dc f", p=128)
        srcu = io["wu_bf"][:, ic * 128:(ic + 1) * 128].rearrange("(dc p) f -> p dc f", p=128)
        k.dma("dma", wg.rearrange("p (dc f) -> p dc f", dc=DC), srcg)
        k.dma("dma", wu.rearrange("p (dc f) -> p dc f", dc=DC), srcu)
        for dc in range(DC):
            k.mm(psG[:, 0:T], wg[:, dc * 128:(dc + 1) * 128], fdc(u_bf, dc),
                 start=(dc == 0), stop=(dc == DC - 1))
            k.mm(psU[:, 0:T], wu[:, dc * 128:(dc + 1) * 128], fdc(u_bf, dc),
                 start=(dc == 0), stop=(dc == DC - 1))
        sg = sgt[:, s * T:(s + 1) * T]
        up = upt[:, s * T:(s + 1) * T]
        k.op("act", lambda sg=sg, ic=ic: nc.scalar.activation(
            sg, psG[:, 0:T], AF.Silu, scale=gs[:, ic:ic + 1]),
            reads=[psG[:, 0:T], gs[:, ic:ic + 1]], writes=[sg])
        k.op("dve", lambda up=up, ic=ic: nc.vector.tensor_scalar_mul(
            up, psU[:, 0:T], us[:, ic:ic + 1]),
            reads=[psU[:, 0:T], us[:, ic:ic + 1]], writes=[up])
        hs = hdn[:, ic * T:(ic + 1) * T]
        k.op("gp", lambda hs=hs, sg=sg, up=up: nc.gpsimd.tensor_mul(hs, sg, up),
             reads=[sg, up], writes=[hs])

    for dco in range(DC):
        s = dco % 2
        wt = wdb2[:, s * 5632:(s + 1) * 5632]
        src = io["wd_bf"][:, dco * 128:(dco + 1) * 128].rearrange("(ic p) f -> p ic f", p=128)
        k.dma("dma", wt.rearrange("p (ic f) -> p ic f", ic=NI), src)
        for ic in range(NI):
            k.mm(psO[s][:, 0:T], wt[:, ic * 128:(ic + 1) * 128], hdn[:, ic * T:(ic + 1) * T],
                 start=(ic == 0), stop=(ic == NI - 1))
        k.op("dve", lambda s=s, dco=dco: nc.vector.tensor_scalar_mul(
            fdc(fY, dco), psO[s][:, 0:T], ds[:, dco:dco + 1]),
            reads=[psO[s][:, 0:T], ds[:, dco:dco + 1]], writes=[fdc(fY, dco)])

    emit_ln(k, bufs, fY, "res", x_sb=x_sb)


def emit_cast_pass(k, src_ap, dst_ap, R, C, sign, stgf, stgb):
    nc = k.nc
    CT = 4096
    i = 0
    for r in range(R // 128):
        for c0 in range(0, C, CT):
            w = min(CT, C - c0)
            s = i % 2
            i += 1
            sf = stgf[s][:, 0:w]
            sb_ = stgb[s][:, 0:w]
            k.dma("dma", sf, src_ap[r * 128:(r + 1) * 128, c0:c0 + w])
            if sign:
                k.op("act", lambda sb_=sb_, sf=sf: nc.scalar.activation(sb_, sf, AF.Sign),
                     reads=[sf], writes=[sb_])
            else:
                k.op("dve", lambda sb_=sb_, sf=sf: nc.vector.tensor_copy(sb_, sf),
                     reads=[sf], writes=[sb_])
            k.dma("dma", dst_ap[r * 128:(r + 1) * 128, c0:c0 + w], sb_)


def alloc_ln(k, stack, bufs):
    nc = k.nc
    def sb(name, shape, dt):
        return stack.enter_context(nc.sbuf_tensor(name, shape, dt))[:]
    def ps(name, shape, dt=F32):
        return stack.enter_context(nc.psum_tensor(name, shape, dt))[:]
    bufs["x_sb"] = sb("x_sb", [128, DC * T], F32)
    bufs["fY"] = sb("fY", [128, DC * T], F32)
    bufs["cast2"] = sb("cast2", [128, 2 * T], BF16)
    bufs["sq2"] = sb("sq2", [128, 2 * T], BF16)
    bufs["rowb"] = sb("rowb", [1, 3 * T], F32)
    bufs["lnt"] = sb("lnt", [128, 2 * T], F32)
    bufs["rb_sb"] = sb("rb_sb", [128, T], F32)
    bufs["mrb_sb"] = sb("mrb_sb", [128, T], F32)
    bufs["ones_bf"] = sb("ones_bf", [128, 1], BF16)
    bufs["eps_sb"] = sb("eps_sb", [128, 2], F32)
    bufs["ones_f1"] = sb("ones_f1", [1, 128], F32)
    bufs["psS"] = ps("psS", [128, 1024])
    bufs["sb"] = sb
    bufs["ps"] = ps


def alloc_a(k, stack, bufs):
    sb = bufs["sb"]; ps = bufs["ps"]
    bufs["fU"] = sb("fU", [128, DC * T], F32)
    bufs["fDT"] = sb("fDT", [128, DC * T], F32)
    bufs["ub2"] = sb("ub2", [128, 2 * T], BF16)
    bufs["dtlow"] = sb("dtlow", [128, T], BF16)
    bufs["bc_sb"] = sb("bc_sb", [32, T], F32)
    bufs["abuf"] = sb("abuf", [128, 2 * T], F32)
    bufs["bxb"] = sb("bxb", [128, 2 * 4 * T], F32)
    bufs["hbuf"] = sb("hbuf", [128, 2 * 4 * T], F32)
    bufs["bcb"] = sb("bcb", [128, 2 * T], F32)
    bufs["ccb"] = sb("ccb", [128, 2 * T], F32)
    bufs["hstate"] = sb("hstate", [128, DC * NS], F32)
    bufs["tmpA"] = bufs["hbuf"][:, 0:2 * T]
    bufs["tmpB"] = bufs["lnt"]
    bufs["xpw_sb"] = sb("xpw_sb", [128, DC * 160], BF16)
    bufs["dtw_sb"] = sb("dtw_sb", [128, D], BF16)
    bufs["dtb_sb"] = sb("dtb_sb", [128, DC], F32)
    bufs["dpar_sb"] = sb("dpar_sb", [128, DC], F32)
    bufs["a_sb"] = sb("a_sb", [128, DC * NS], F32)
    bufs["psA"] = ps("psA", [128, 512])
    bufs["psB"] = ps("psB", [128, 512])
    bufs["psC0"] = ps("psC0", [128, 512])
    bufs["psC1"] = ps("psC1", [128, 512])


def alloc_b(k, stack, bufs):
    sb = bufs["sb"]; ps = bufs["ps"]
    bufs["u_bf"] = sb("u_bf", [128, DC * T], BF16)
    bufs["hf"] = sb("hf", [128, 64 * T], BF16)
    bufs["wub"] = sb("wub", [128, 2 * 2048], BF16)
    bufs["wdb"] = sb("wdb", [128, 2 * 8192], BF16)
    bufs["ups_sb"] = sb("ups_sb", [128, 64], F32)
    bufs["dns_sb"] = sb("dns_sb", [128, DC], F32)
    bufs["os_sb"] = sb("os_sb", [128, 1], F32)
    bufs["psA"] = ps("psA", [128, 512])
    bufs["psB"] = ps("psB", [128, 512])
    bufs["psC0"] = ps("psC0", [128, 512])
    bufs["psC1"] = ps("psC1", [128, 512])
    hfF = bufs["hf"].bitcast(F32)
    bufs["stgf"] = [hfF[:, 0:4096], hfF[:, 4096:8192]]
    bufs["stgb"] = [bufs["hf"][:, 16384:20480], bufs["hf"][:, 20480:24576]]


def alloc_c(k, stack, bufs):
    sb = bufs["sb"]; ps = bufs["ps"]
    bufs["u_bf"] = sb("u_bf", [128, DC * T], BF16)
    bufs["qfb"] = sb("qfb", [128, DC * T], BF16)
    bufs["kfb"] = sb("kfb", [128, DC * T], BF16)
    bufs["att"] = sb("att", [128, DC * T], BF16)
    bufs["vtb"] = sb("vtb", [128, 4 * NH * 129], BF16)
    bufs["kvst"] = sb("kvst", [128, NH * 129], F32)
    bufs["kvst_bf"] = sb("kvst_bf", [128, NH * 129], BF16)
    bufs["kt_sb"] = sb("kt_sb", [128, 2 * 128], BF16)
    bufs["atb"] = sb("atb", [128, 128], BF16)
    bufs["otb"] = sb("otb", [128, 2 * 128], BF16)
    bufs["wqb"] = sb("wqb", [128, 2 * 2048], BF16)
    bufs["wvb"] = sb("wvb", [128, 8 * 512], BF16)
    bufs["tmpR"] = sb("tmpR", [128, T], F32)
    bufs["tmpE"] = sb("tmpE", [128, T], F32)
    bufs["zb"] = sb("zb", [128, 2], F32)
    bufs["mask_sb"] = sb("mask_sb", [128, 128], F32)
    bufs["ident_bf"] = sb("ident_bf", [128, 128], BF16)
    bufs["ones_bf_row"] = sb("ones_bf_row", [1, 128], BF16)
    bufs["bq_sb"] = sb("bq_sb", [128, NH], F32)
    bufs["bk_sb"] = sb("bk_sb", [128, NH], F32)
    bufs["bo_sb"] = sb("bo_sb", [128, NH], F32)
    bufs["bv_bf"] = sb("bv_bf", [1, D], BF16)
    bufs["psA"] = ps("psA", [128, 1024])
    bufs["psB"] = ps("psB", [128, 1024])
    bufs["psT"] = ps("psT", [128, 2048], BF16)
    qfF = bufs["qfb"].bitcast(F32)
    kfF = bufs["kfb"].bitcast(F32)
    bufs["stgf"] = [qfF[:, 0:4096], kfF[:, 0:4096]]
    bufs["stgb"] = [bufs["att"][:, 0:4096], bufs["att"][:, 4096:8192]]


def alloc_d(k, stack, bufs):
    sb = bufs["sb"]; ps = bufs["ps"]
    bufs["u_bf"] = sb("u_bf", [128, DC * T], BF16)
    bufs["hdn"] = sb("hdn", [128, 44 * T], BF16)
    bufs["wgb"] = sb("wgb", [128, 2 * 2048], BF16)
    bufs["wub2"] = sb("wub2", [128, 2 * 2048], BF16)
    bufs["wdb2"] = sb("wdb2", [128, 2 * 5632], BF16)
    bufs["gs_sb"] = sb("gs_sb", [128, 44], F32)
    bufs["us_sb"] = sb("us_sb", [128, 44], F32)
    bufs["ds_sb"] = sb("ds_sb", [128, DC], F32)
    bufs["sgt"] = sb("sgt", [128, 2 * T], F32)
    bufs["upt"] = sb("upt", [128, 2 * T], F32)
    bufs["psA"] = ps("psA", [128, 512])
    bufs["psB"] = ps("psB", [128, 512])
    bufs["psC0"] = ps("psC0", [128, 512])
    bufs["psC1"] = ps("psC1", [128, 512])
    hdF = bufs["hdn"].bitcast(F32)
    bufs["stgf"] = [hdF[:, 0:4096], hdF[:, 4096:8192]]
    bufs["stgb"] = [bufs["hdn"][:, 16384:20480], bufs["hdn"][:, 20480:22528]]


def emit_ln_setup(k, bufs, io):
    k.dma("dma", bufs["ones_bf"], io["c_ones_col"])
    k.dma("dma", bufs["ones_f1"], io["c_ones_row"])
    k.dma("dma", bufs["eps_sb"], io["c_eps"])


def emit_a_setup(k, bufs, io):
    nc = k.nc
    stg = bufs["bxb"]
    src = io["xpw_t"].rearrange("(dc p) f -> p dc f", p=128)
    k.dma("dma", stg[:, 0:DC * 160].rearrange("p (dc f) -> p dc f", dc=DC, f=160), src)
    k.op("dve", lambda: nc.vector.tensor_copy(bufs["xpw_sb"], stg[:, 0:DC * 160]),
         reads=[stg[:, 0:DC * 160]], writes=[bufs["xpw_sb"]])
    stg2 = bufs["hbuf"]
    k.dma("dma", stg2[:, 0:D], io["dtw_t"])
    k.op("dve", lambda: nc.vector.tensor_copy(bufs["dtw_sb"], stg2[:, 0:D]),
         reads=[stg2[:, 0:D]], writes=[bufs["dtw_sb"]])
    k.dma("dma", bufs["dtb_sb"], io["dt_b"])
    k.dma("dma", bufs["dpar_sb"], io["d_par"])
    k.dma("dma", bufs["a_sb"], io["a_log"])
    k.op("act", lambda: nc.scalar.activation(bufs["a_sb"], bufs["a_sb"], AF.Exp),
         reads=[bufs["a_sb"]], writes=[bufs["a_sb"]])
    k.op("dve", lambda: nc.vector.tensor_scalar_mul(bufs["a_sb"], bufs["a_sb"], -1.0),
         reads=[bufs["a_sb"]], writes=[bufs["a_sb"]])
    k.op("dve", lambda: nc.vector.memset(bufs["hstate"], 0.0), writes=[bufs["hstate"]])


def emit_b_setup(k, bufs, io):
    k.dma("dma", bufs["ups_sb"], io["up_s"])
    k.dma("dma", bufs["dns_sb"], io["dn_s"])
    k.dma("dma", bufs["os_sb"], io["c_os"])
    emit_cast_pass(k, io["wup_t"], io["wup_bf"], D, 8192, True,
                   bufs["stgf"], bufs["stgb"])
    emit_cast_pass(k, io["wdn_t"], io["wdn_bf"], 8192, D, True,
                   bufs["stgf"], bufs["stgb"])


def emit_c_setup(k, bufs, io):
    nc = k.nc
    for n in ("bq", "bk", "bo"):
        k.dma("dma", bufs[n + "_sb"], io[n])
    k.dma("dma", bufs["mask_sb"], io["c_mask"])
    k.dma("dma", bufs["ident_bf"], io["c_ident"])
    k.dma("dma", bufs["ones_bf_row"], io["c_ones_bfr"])
    bvstg = bufs["stgf"][0][0:1, 0:D]
    k.dma("dma", bvstg, io["bv"])
    k.op("dve", lambda: nc.vector.tensor_copy(bufs["bv_bf"], bvstg),
         reads=[bvstg], writes=[bufs["bv_bf"]])
    k.op("dve", lambda: nc.vector.memset(bufs["kvst"], 0.0), writes=[bufs["kvst"]])
    vcols = bufs["vtb"].rearrange("p (cc h e) -> p cc h e", cc=4, h=NH)[:, :, :, 128:129]
    k.op("dve", lambda: nc.vector.memset(vcols, 1.0), writes=[bufs["vtb"]])
    for w in ("wq", "wk", "wv", "wo"):
        emit_cast_pass(k, io[w + "_t"], io[w + "_bf"], D, D, False,
                       bufs["stgf"], bufs["stgb"])


def emit_d_setup(k, bufs, io):
    k.dma("dma", bufs["gs_sb"], io["g_s"])
    k.dma("dma", bufs["us_sb"], io["u_s"])
    k.dma("dma", bufs["ds_sb"], io["d_s"])
    emit_cast_pass(k, io["wg_t"], io["wg_bf"], D, 5632, False, bufs["stgf"], bufs["stgb"])
    emit_cast_pass(k, io["wu_t"], io["wu_bf"], D, 5632, False, bufs["stgf"], bufs["stgb"])
    emit_cast_pass(k, io["wd_t"], io["wd_bf"], 5632, D, False, bufs["stgf"], bufs["stgb"])


def _dram(nc, name, shape, dt=F32, kind="ExternalInput"):
    return nc.dram_tensor(name, shape, dt, kind=kind).ap()


def build_block_program(kind):
    nc = bass.Bass("TRN2", target_bir_lowering=False)
    io = {}
    io["x_fm"] = _dram(nc, "x_fm", [D, LQ])
    io["c_ones_col"] = _dram(nc, "c_ones_col", [128, 1], BF16)
    io["c_ones_row"] = _dram(nc, "c_ones_row", [1, 128])
    io["c_eps"] = _dram(nc, "c_eps", [128, 2])
    out = _dram(nc, "y_fm", [D, LQ], F32, kind="ExternalOutput")
    if kind == "a":
        io["xpw_t"] = _dram(nc, "xpw_t", [D, 160])
        io["dtw_t"] = _dram(nc, "dtw_t", [128, D])
        io["dt_b"] = _dram(nc, "dt_b", [128, DC])
        io["d_par"] = _dram(nc, "d_par", [128, DC])
        io["a_log"] = _dram(nc, "a_log", [128, DC * NS])
        io["bc_dram"] = [_dram(nc, "bc_dram0", [32, T], F32, "Internal"),
                         _dram(nc, "bc_dram1", [32, T], F32, "Internal")]
    elif kind == "b":
        io["wup_t"] = _dram(nc, "wup_t", [D, 8192])
        io["wdn_t"] = _dram(nc, "wdn_t", [8192, D])
        io["up_s"] = _dram(nc, "up_s", [128, 64])
        io["dn_s"] = _dram(nc, "dn_s", [128, DC])
        io["c_os"] = _dram(nc, "c_os", [128, 1])
        io["wup_bf"] = _dram(nc, "wup_bf", [D, 8192], BF16, "Internal")
        io["wdn_bf"] = _dram(nc, "wdn_bf", [8192, D], BF16, "Internal")
    elif kind == "c":
        for w in ("wq", "wk", "wv", "wo"):
            io[w + "_t"] = _dram(nc, w + "_t", [D, D])
            io[w + "_bf"] = _dram(nc, w + "_bf", [D, D], BF16, "Internal")
        io["bq"] = _dram(nc, "bq", [128, NH])
        io["bk"] = _dram(nc, "bk", [128, NH])
        io["bo"] = _dram(nc, "bo", [128, NH])
        io["bv"] = _dram(nc, "bv", [1, D])
        io["c_mask"] = _dram(nc, "c_mask", [128, 128])
        io["c_ident"] = _dram(nc, "c_ident", [128, 128], BF16)
        io["c_ones_bfr"] = _dram(nc, "c_ones_bfr", [1, 128], BF16)
    elif kind == "d":
        io["wg_t"] = _dram(nc, "wg_t", [D, 5632])
        io["wu_t"] = _dram(nc, "wu_t", [D, 5632])
        io["wd_t"] = _dram(nc, "wd_t", [5632, D])
        io["g_s"] = _dram(nc, "g_s", [128, 44])
        io["u_s"] = _dram(nc, "u_s", [128, 44])
        io["d_s"] = _dram(nc, "d_s", [128, DC])
        io["wg_bf"] = _dram(nc, "wg_bf", [D, 5632], BF16, "Internal")
        io["wu_bf"] = _dram(nc, "wu_bf", [D, 5632], BF16, "Internal")
        io["wd_bf"] = _dram(nc, "wd_bf", [5632, D], BF16, "Internal")

    with ExitStack() as stack:
        k = K(nc, stack)
        bufs = {}
        alloc_ln(k, stack, bufs)
        {"a": alloc_a, "b": alloc_b, "c": alloc_c, "d": alloc_d}[kind](k, stack, bufs)
        if kind == "a":
            bufs["bc_dram"] = io["bc_dram"]
        emit_ln_setup(k, bufs, io)
        {"a": emit_a_setup, "b": emit_b_setup, "c": emit_c_setup,
         "d": emit_d_setup}[kind](k, bufs, io)
        emit = {"a": lambda kk, bb, cc: emit_block_a(kk, bb, cc),
                "b": lambda kk, bb, cc: emit_block_b(kk, bb, io, cc),
                "c": lambda kk, bb, cc: emit_block_c(kk, bb, io, cc),
                "d": lambda kk, bb, cc: emit_block_d(kk, bb, io, cc)}[kind]
        st_tag = None
        for c in range(NT):
            src = io["x_fm"].rearrange("(dc p) t -> p dc t", p=128)[:, :, c * T:(c + 1) * T]
            k.dma("dma", bufs["x_sb"].rearrange("p (dc t) -> p dc t", dc=DC), src)
            emit(k, bufs, c)
            dst = out.rearrange("(dc p) t -> p dc t", p=128)[:, :, c * T:(c + 1) * T]
            st_tag = k.dma("dma", dst, bufs["x_sb"].rearrange("p (dc t) -> p dc t", dc=DC))
        k.wait("sync", st_tag)
        for i in range(k.NPOOL):
            k.wait("sync", (f"gdma{i}", k.cnt[f"gdma{i}"]))
            k.wait("sync", (f"dma{i}", k.cnt[f"dma{i}"]))
    return nc


def _relay_vec(v):
    return np.ascontiguousarray(v.reshape(-1, 128).T)


_CONSTS = None


def _consts():
    global _CONSTS
    if _CONSTS is None:
        _CONSTS = {
            "c_ones_col": np.ones((128, 1), BF),
            "c_ones_row": np.ones((1, 128), np.float32),
            "c_eps": np.tile(np.array([[1e-5, 1e-6]], np.float32), (128, 1)),
        }
    return _CONSTS


def make_inmaps(kind, inputs, x_percore):
    common = dict(_consts())
    if kind == "a":
        common.update({
            "xpw_t": np.ascontiguousarray(inputs["L0_attn_x_proj_w"].T),
            "dtw_t": np.ascontiguousarray(inputs["L0_attn_dt_proj_w"].T),
            "dt_b": _relay_vec(inputs["L0_attn_dt_proj_b"]),
            "d_par": _relay_vec(inputs["L0_attn_D_param"]),
            "a_log": np.ascontiguousarray(
                inputs["L0_attn_A_log"].reshape(DC, 128, NS).transpose(1, 0, 2)
                .reshape(128, DC * NS)),
        })
    elif kind == "b":
        common.update({
            "wup_t": np.ascontiguousarray(inputs["L0_ffn_up_weight"].T),
            "wdn_t": np.ascontiguousarray(inputs["L0_ffn_down_weight"].T),
            "up_s": _relay_vec(inputs["L0_ffn_up_scale"]),
            "dn_s": _relay_vec(inputs["L0_ffn_down_scale"]),
            "c_os": np.tile(inputs["L0_ffn_output_scale"].reshape(1, 1), (128, 1)),
        })
    elif kind == "c":
        common.update({
            "wq_t": np.ascontiguousarray(inputs["L1_attn_Wq"]),
            "wk_t": np.ascontiguousarray(inputs["L1_attn_Wk"]),
            "wv_t": np.ascontiguousarray(inputs["L1_attn_Wv"]),
            "wo_t": np.ascontiguousarray(inputs["L1_attn_Wo"]),
            "bq": _relay_vec(inputs["L1_attn_bq"]),
            "bk": _relay_vec(inputs["L1_attn_bk"]),
            "bo": _relay_vec(inputs["L1_attn_bo"]),
            "bv": inputs["L1_attn_bv"].reshape(1, D),
            "c_mask": np.triu(np.ones((128, 128), np.float32)),
            "c_ident": np.eye(128, dtype=BF),
            "c_ones_bfr": np.ones((1, 128), BF),
        })
    elif kind == "d":
        common.update({
            "wg_t": np.ascontiguousarray(inputs["L1_ffn_gate_w"].T),
            "wu_t": np.ascontiguousarray(inputs["L1_ffn_up_w"].T),
            "wd_t": np.ascontiguousarray(inputs["L1_ffn_down_w"].T),
            "g_s": _relay_vec(inputs["L1_ffn_gate_s"]),
            "u_s": _relay_vec(inputs["L1_ffn_up_s"]),
            "d_s": _relay_vec(inputs["L1_ffn_down_s"]),
        })
    maps = []
    for cid in range(8):
        m = dict(common)
        m["x_fm"] = x_percore[cid]
        maps.append(m)
    return maps


def kernel(**inputs):
    x = inputs["L0_attn_x"]
    x_pc = [np.ascontiguousarray(x[cid].T) for cid in range(8)]
    for kind in ("a", "b", "c", "d"):
        nc = build_block_program(kind)
        in_maps = make_inmaps(kind, inputs, x_pc)
        res = run_bass_kernel_spmd(nc, in_maps, core_ids=list(range(8)))
        x_pc = [np.asarray(r["y_fm"]) for r in res.results]
    return np.stack([np.ascontiguousarray(xc.T) for xc in x_pc])
